# version 18
# baseline (speedup 1.0000x reference)
import sys
sys.path.insert(0, "/opt/trn_rl_repo")
import numpy as np
import ml_dtypes
import concourse.bass as bass
import concourse.mybir as mybir
from concourse.bass_utils import run_bass_kernel_spmd

F32 = mybir.dt.float32
BF16 = mybir.dt.bfloat16
AF = mybir.ActivationFunctionType
ALU = mybir.AluOpType
AX = mybir.AxisListType

D = 2048
T = 2048
H = 16
NCH = 16
AR = 1024
QSCALE = 1.0 / float(np.sqrt(128.0))
EPS = 1e-5

ENGS = ("sync", "tensor", "vector", "scalar", "gpsimd")


class Sem:
    __slots__ = ("name", "h", "n")

    def __init__(self, name):
        self.name = name
        self.h = None
        self.n = 0


class Plan:
    def __init__(self):
        self.q = {e: [] for e in ENGS}

    def op(self, eng, fn, waits=(), inc=None, n=1):
        w = []
        for t in waits:
            if t is None:
                continue
            s, v = t
            if s is not None and v is not None and v > 0:
                w.append((s, v))
        self.q[eng].append((fn, w, inc, n))
        if inc is not None:
            inc.n += n
            return inc.n
        return None


LAST_PLAN = None


def build_nc(phases=("p1", "attn", "ffn"), dump=()):
    global LAST_PLAN
    nc = bass.Bass("TRN2", target_bir_lowering=False)

    x_d = nc.dram_tensor("x", [D, T], F32, kind="ExternalInput").ap()
    wq_d = nc.dram_tensor("wq", [D, D], F32, kind="ExternalInput").ap()
    wk_d = nc.dram_tensor("wk", [D, D], F32, kind="ExternalInput").ap()
    wv_d = nc.dram_tensor("wv", [D, D], F32, kind="ExternalInput").ap()
    wo_d = nc.dram_tensor("wo", [D, D], F32, kind="ExternalInput").ap()
    wa_d = nc.dram_tensor("wa", [D, AR], F32, kind="ExternalInput").ap()
    wb_d = nc.dram_tensor("wb", [AR, D], F32, kind="ExternalInput").ap()
    ww_d = nc.dram_tensor("ww", [D, D], F32, kind="ExternalInput").ap()
    bias_d = nc.dram_tensor("biasr", [128, 16], F32, kind="ExternalInput").ap()
    constb_d = nc.dram_tensor("constb", [128, 2304], BF16, kind="ExternalInput").ap()
    constf_d = nc.dram_tensor("constf", [128, 512], F32, kind="ExternalInput").ap()

    out_d = nc.dram_tensor("out", [D, T], F32, kind="ExternalOutput").ap()
    q_hbm = nc.dram_tensor("q_scr", [D, T], BF16).ap()
    wo_bf = nc.dram_tensor("wo_bf_scr", [D, D], BF16).ap()
    x2_hbm = nc.dram_tensor("x2_scr", [D, T], F32).ap()

    dump_d = {}
    for name in dump:
        if name == "ffn1":
            dump_d["arr"] = nc.dram_tensor("dump_arr", [64, T], F32,
                                           kind="ExternalOutput").ap()
            dump_d["routes"] = nc.dram_tensor("dump_routes", [64, T], BF16,
                                              kind="ExternalOutput").ap()
            dump_d["u"] = nc.dram_tensor("dump_u", [AR, T], BF16,
                                         kind="ExternalOutput").ap()
            dump_d["atm"] = nc.dram_tensor("dump_atm", [128, 1024], F32,
                                           kind="ExternalOutput").ap()
            continue
        if name == "lnd":
            dump_d["abc"] = nc.dram_tensor("dump_abc", [128, 4096], F32,
                                           kind="ExternalOutput").ap()
            dump_d["lnr"] = nc.dram_tensor("dump_lnr", [2, 2048], F32,
                                           kind="ExternalOutput").ap()
            continue
        dt = F32 if name in ("x2",) else BF16
        dump_d[name] = nc.dram_tensor("dump_" + name, [D, T], dt,
                                      kind="ExternalOutput").ap()

    p = Plan()
    s_pe = Sem("s_pe")
    s_act = Sem("s_act")
    s_dve = Sem("s_dve")
    SC = Sem("sc")
    SX = [Sem(f"sx{i}") for i in range(8)]
    SQT = [Sem(f"sqt{i}") for i in range(2)]
    SQL = [Sem(f"sql{i}") for i in range(4)]
    SWP = [Sem(f"swp{i}") for i in range(2)]
    SWV = Sem("swv")
    SWA = Sem("swa")
    SWB = Sem("swb")
    SWW = Sem("sww")
    SDBG = Sem("sdbg")
    SBI = Sem("sbi")
    SBO = [Sem("sbo0"), Sem("sbo1")]
    sems = ([s_pe, s_act, s_dve, SC, SWV, SWA, SWB, SWW, SDBG, SBI]
            + SBO + SX + SQT + SQL + SWP)

    import contextlib
    stack = contextlib.ExitStack()

    def sb(name, shape, dt):
        return stack.enter_context(nc.sbuf_tensor(name, shape, dt))

    def psum(name, shape, dt):
        return stack.enter_context(nc.psum_tensor(name, shape, dt))

    with stack:
        BIG1 = sb("big1", [128, 16384], BF16)
        BIG2 = sb("big2", [128, 32768], BF16)
        BIG3 = sb("big3", [128, 32768], BF16)
        WST = sb("wst", [128, 5120], BF16)
        XB = sb("xb", [128, 4096], F32)
        SMALL = sb("small", [128, 1024], F32)
        ROWS = sb("rows", [128, 2048], F32)
        TOPK = sb("topk", [128, 768], F32)
        BIASR = sb("biasr_sb", [128, 16], F32)
        CONSTB = sb("constb_sb", [128, 2304], BF16)
        CONSTF = sb("constf_sb", [128, 512], F32)

        def ln1v(c, lo, n):
            return BIG1[:, c * 1024 + lo: c * 1024 + lo + n]

        def zv(hc):
            return BIG1[:, hc * 512: (hc + 1) * 512]

        QSL = [BIG1[:, 8192:8704], BIG1[:, 8704:9216],
               BIG1[:, 12288:12800], BIG1[:, 12800:13312]]
        PT = [BIG1[:, 9216 + i * 512: 9216 + (i + 1) * 512] for i in range(6)]

        def ln2v(c):
            return BIG1[:, c * 512: (c + 1) * 512]

        def vecv(at):
            return BIG1[:, 8192 + at * 512: 8192 + (at + 1) * 512]

        A1 = BIG1[:, 12288:12800]
        A2 = BIG1[:, 12800:13312]
        A3 = BIG1[:, 13312:13824]
        A4 = BIG1[:, 13824:14336]
        A5 = BIG1[:, 14336:14848]
        A6 = BIG1[:, 14848:15360]
        A7 = BIG1[:, 15360:15616]
        A8 = BIG1[:, 15616:16128]

        def ksb(h, lo=0, n=2048):
            return BIG2[:, h * 2048 + lo: h * 2048 + lo + n]

        def wwsb(c):
            return BIG2[:, c * 2048: (c + 1) * 2048]

        def vsb(mt, lo=0, n=2048):
            return BIG3[:, mt * 2048 + lo: mt * 2048 + lo + n]

        def wasb(c):
            return BIG3[:, c * 1024: (c + 1) * 1024]

        def wbsb(cc):
            return BIG3[:, 16384 + cc * 2048: 16384 + (cc + 1) * 2048]

        WPC = [WST[:, 0:2048], WST[:, 2048:4096]]
        WVSL = WST[:, 0:4096]
        QST = [WST[:, 4096 + i * 512: 4096 + (i + 1) * 512] for i in range(2)]

        XS = [XB[:, i * 512: (i + 1) * 512] for i in range(8)]
        ABC = SMALL[:, 0:512]
        LNS = SMALL[:, 512:1024]
        RW = [ROWS[0:1, i * 512:(i + 1) * 512] for i in range(4)]
        RWF = [ROWS[:, i * 512:(i + 1) * 512] for i in range(4)]

        ident = CONSTB[:, 0:128]
        tri = CONSTB[:, 128:256]
        selA = [CONSTB[:, 256 + c * 64: 256 + (c + 1) * 64] for c in range(8)]
        selExp = [CONSTB[0:64, 768 + c * 128: 768 + (c + 1) * 128] for c in range(8)]
        selR = CONSTB[:, 1792:1808]
        selTile = CONSTB[0:16, 1808:1936]
        ones_col = CONSTB[:, 1936:1937]
        trineg = CONSTB[:, 2048:2176]
        ones128 = CONSTB[:, 2176:2304]
        onesf = CONSTF[0:1, 0:128]
        identf64 = CONSTF[0:64, 128:192]
        epsap = CONSTF[:, 192:193]
        identf128 = CONSTF[:, 256:384]
        ones128f = CONSTF[:, 384:512]

        P0 = psum("p0", [128, 512], F32)
        P1 = psum("p1", [128, 512], F32)
        P2 = psum("p2", [128, 512], F32)
        P3 = psum("p3", [128, 512], F32)
        P4 = psum("p4", [128, 512], F32)
        P5 = psum("p5", [128, 512], F32)
        P6 = psum("p6", [128, 512], F32)
        P7 = psum("p7", [128, 512], F32)

        for s in sems:
            s.h = stack.enter_context(nc.semaphore(s.name))

        def dma(eng, out, in_, sem, waits=()):
            v = p.op(eng, lambda e, o=out, i=in_: e.dma_start(out=o, in_=i),
                     waits=waits, inc=sem, n=16)
            return (sem, v)

        def mm(out, lhsT, rhs, start, stop, waits=(), inc=None):
            return p.op("tensor",
                        lambda e, o=out, l=lhsT, r=rhs, st=start, sp=stop:
                        e.matmul(o, l, r, start=st, stop=sp),
                        waits=waits, inc=inc)

        def tr(out, in_, idn, waits=()):
            return p.op("tensor",
                        lambda e, o=out, i=in_, dd=idn: e.transpose(o, i, dd),
                        waits=waits, inc=s_pe)

        def acp(out, in_, waits=(), scale=None):
            if scale is None:
                return p.op("scalar", lambda e, o=out, i=in_: e.copy(o, i),
                            waits=waits, inc=s_act)
            return p.op("scalar",
                        lambda e, o=out, i=in_, sc=scale:
                        e.activation(o, i, AF.Identity, scale=sc),
                        waits=waits, inc=s_act)

        def aspace(waits=()):
            return p.op("scalar",
                        lambda e: e.copy(ROWS[0:1, 2040:2048], ROWS[0:1, 2032:2040]),
                        waits=waits, inc=s_act)

        def aact(out, in_, func, waits=(), scale=None, bias=None):
            kw = {}
            if scale is not None:
                kw["scale"] = scale
            if bias is not None:
                kw["bias"] = bias
            return p.op("scalar",
                        lambda e, o=out, i=in_, f=func, k=kw: e.activation(o, i, f, **k),
                        waits=waits, inc=s_act)

        def vtt(out, in0, in1, aop, waits=()):
            return p.op("vector",
                        lambda e, o=out, a=in0, b=in1, op=aop: e.tensor_tensor(o, a, b, op),
                        waits=waits, inc=s_dve)

        def vcp(out, in_, waits=()):
            return p.op("vector",
                        lambda e, o=out, i=in_: e.tensor_copy(o, i),
                        waits=waits, inc=s_dve)

        def vts(out, in0, sc1, aop, waits=()):
            return p.op("vector",
                        lambda e, o=out, a=in0, s1=sc1, op=aop:
                        e.tensor_scalar(o, a, s1, None, op),
                        waits=waits, inc=s_dve)

        def vstt(out, in0, sc, in1, op0, op1, waits=()):
            return p.op("vector",
                        lambda e, o=out, a=in0, s1=sc, b=in1, o0=op0, o1=op1:
                        e.scalar_tensor_tensor(o, a, s1, b, o0, o1),
                        waits=waits, inc=s_dve)

        def vmax(out, in_, waits=()):
            return p.op("vector",
                        lambda e, o=out, i=in_: e.reduce_max(o, i, axis=AX.X),
                        waits=waits, inc=s_dve)

        def vrecip(out, in_, waits=()):
            return p.op("vector",
                        lambda e, o=out, i=in_: e.reciprocal(o, i),
                        waits=waits, inc=s_dve)

        dma("sync", CONSTB[:, :], constb_d, SC)
        dma("sync", CONSTF[:, :], constf_d, SC)
        dma("sync", BIASR[:, :], bias_d, SC)
        consts_ready = (SC, SC.n)

        xs_free = [None] * 8
        xs_free2 = [None] * 8
        wpc_free = [None, None]
        qst_free = [None, None]
        p4_free = None
        p5_free = None
        p6_free = None
        p2_free = None
        mmfree = {id(P0): None, id(P1): None}
        abc_free = None
        bbc_free = None
        rows_free = [None, None, None, None]
        qstore = {}
        x2store = {}

        def lnrows(v_sum, v_sq):
            nonlocal abc_free, bbc_free, p4_free, p5_free
            v_m = acp(RWF[0], P4[:, :], waits=[(s_pe, v_sum), rows_free[0]],
                      scale=1.0 / D)
            v_ms = acp(RWF[1], P5[:, :], waits=[(s_pe, v_sq), rows_free[1]],
                       scale=1.0 / D)
            p4_free = (s_act, v_m)
            p5_free = (s_act, v_ms)
            v_m2 = aact(RWF[2], RWF[0], AF.Square,
                        waits=[(s_act, v_m), rows_free[2], abc_free])
            v_var = vtt(RWF[3], RWF[1], RWF[2], ALU.subtract,
                        waits=[(s_act, v_m2), rows_free[3], bbc_free])
            rows_free[1] = (s_dve, v_var)
            v_lnv = aact(RWF[1], RWF[3], AF.Ln, waits=[(s_dve, v_var)],
                         bias=epsap)
            v_rstd = aact(RWF[2], RWF[1], AF.Exp, waits=[(s_act, v_lnv)],
                          scale=-0.5)
            rows_free[2] = (s_act, v_rstd)
            v_s = vtt(RWF[3], RWF[0], RWF[2], ALU.mult, waits=[(s_act, v_rstd)])
            rows_free[0] = (s_dve, v_s)
            rows_free[3] = (s_dve, v_s)
            return (s_act, v_rstd), (s_dve, v_s)

        ln1_done = [None, None]
        qkv_done = [None, None]

        if "p1" in phases:
            ln1_ready = [[None] * NCH, [None] * NCH]
            wcv_stage = [SMALL.bitcast(BF16)[:, i * 512:(i + 1) * 512] for i in range(2)]
            wcv_state = {"n": 0, "stf": [None, None], "out": None}

            def emit_wo_convert(n):
                for _ in range(n):
                    i = wcv_state["n"]
                    if i >= 64:
                        return
                    cch, hlf = i // 4, i % 4
                    sl = i % 4
                    b = i % 2
                    v_l = dma("sync", XS[sl],
                              wo_d[cch * 128:(cch + 1) * 128,
                                   hlf * 512:(hlf + 1) * 512],
                              SX[sl], waits=[xs_free[sl]])
                    v_cp = vcp(wcv_stage[b], XS[sl],
                               waits=[v_l, wcv_state["stf"][b]])
                    xs_free[sl] = (s_dve, v_cp)
                    v_st = dma("scalar",
                               wo_bf[cch * 128:(cch + 1) * 128,
                                     hlf * 512:(hlf + 1) * 512],
                               wcv_stage[b], SBO[b], waits=[(s_dve, v_cp)])
                    wcv_state["stf"][b] = v_st
                    wcv_state["out"] = v_st
                    wcv_state["n"] = i + 1

            def emit_stats1(half, tb, cs=range(NCH)):
                nonlocal p4_free, p5_free, v_sum1, v_sq1
                t0 = half * 1024
                tc = t0 + tb * 512
                for c in cs:
                    sl = 4 + c % 4
                    ssl = c % 2
                    v_ld = dma("sync", XS[sl],
                               x_d[c * 128:(c + 1) * 128, tc:tc + 512],
                               SX[sl], waits=[xs_free[sl], xs_free2[sl]])
                    v_s2 = aact(XS[ssl], XS[sl], AF.Square,
                                waits=[v_ld, consts_ready, xs_free[ssl]])
                    v_sum1 = mm(P4[:, :], ones128f, XS[sl], c == 0, c == NCH - 1,
                                waits=[v_ld] + ([p4_free, consts_ready] if c == 0 else []),
                                inc=s_pe)
                    xs_free[sl] = (s_pe, v_sum1)
                    xs_free2[sl] = (s_act, v_s2)
                    v_sq1 = mm(P5[:, :], ones128f, XS[ssl], c == 0, c == NCH - 1,
                               waits=[(s_act, v_s2)] + ([p5_free] if c == 0 else []),
                               inc=s_pe)
                    xs_free[ssl] = (s_pe, v_sq1)
                return v_sum1, v_sq1

            v_sum1 = v_sq1 = None
            wob_out = None
            for half in range(2):
                wqb1_virgin = [True]
                t0 = half * 1024
                prev_qkv = qkv_done[1 - half]
                last_apply = None
                for tb in range(2):
                    tc = t0 + tb * 512
                    loc = tb * 512
                    v_sum, v_sq = emit_stats1(half, tb)
                    v_abc, v_bbc = lnrows(v_sum, v_sq)
                    for c in range(NCH):
                        sl = 4 + c % 4
                        v_ld = dma("sync", XS[sl],
                                   x_d[c * 128:(c + 1) * 128, tc:tc + 512],
                                   SX[sl], waits=[xs_free[sl], xs_free2[sl]])
                        scr = 2 + c % 2
                        v_t1 = vtt(XS[scr], XS[sl], RWF[2], ALU.mult,
                                   waits=[v_ld, v_abc, xs_free[scr]])
                        xs_free[sl] = (s_dve, v_t1)
                        wt2 = [v_bbc, (s_dve, v_t1)]
                        if prev_qkv is not None and c == 0:
                            wt2.append((s_pe, prev_qkv))
                        v_t2 = vtt(ln1v(c, loc, 512), XS[scr], RWF[3], ALU.subtract,
                                   waits=wt2)
                        xs_free[scr] = (s_dve, v_t2)
                        last_apply = v_t2
                        ln1_ready[half][c] = (s_dve, v_t2)
                    abc_free = (s_dve, last_apply)
                    bbc_free = (s_dve, last_apply)
                ln1_done[half] = (s_dve, last_apply)
                if "ln1d" in dump_d:
                    for c in range(NCH):
                        dma("sync",
                            dump_d["ln1d"][c * 128:(c + 1) * 128,
                                           half * 1024:(half + 1) * 1024],
                            ln1v(c, 0, 1024), SDBG,
                            waits=[ln1_done[half], (SDBG, SDBG.n)])

                v_last = None
                WQB2 = [WST[:, 0:4096], ROWS.bitcast(BF16)[:, 0:4096]]
                for proj, wd in (("k", wk_d), ("q", wq_d)):
                    wre = wd.rearrange("(c p) he -> p c he", p=128)
                    for pg in range(H // 2):
                        emit_wo_convert(2)
                        buf = pg % 2
                        wal = []
                        if buf == 1 and wqb1_virgin[0]:
                            wal = [ln1_ready[half][NCH - 1]]
                            wqb1_virgin[0] = False
                        v_w = dma("gpsimd",
                                  WQB2[buf].rearrange("p (c f) -> p c f", f=256),
                                  wre[:, :, pg * 256:(pg + 1) * 256],
                                  SWP[buf], waits=[wpc_free[buf]] + wal)
                        for hti in range(2):
                            ht = 2 * pg + hti
                            for ts in range(2):
                                tc = t0 + ts * 512
                                loc = ts * 512
                                bank = [P0, P1][(ht * 2 + ts) % 2]
                                vg = None
                                for c in range(NCH):
                                    w = [ln1_ready[half][c]]
                                    if c == 0:
                                        w += [v_w, mmfree[id(bank)]]
                                    vg = mm(bank[:, :],
                                            WQB2[buf][:, c * 256 + hti * 128:
                                                      c * 256 + hti * 128 + 128],
                                            ln1v(c, loc, 512), c == 0, c == NCH - 1,
                                            waits=w, inc=s_pe if c == NCH - 1 else None)
                                v_last = vg
                                if proj == "k":
                                    v_e = acp(ksb(ht, tc, 512), bank[:, :],
                                              waits=[(s_pe, vg)])
                                    mmfree[id(bank)] = (s_act, v_e)
                                else:
                                    qs = (ht * 2 + ts) % 2
                                    v_e = acp(QST[qs], bank[:, :],
                                              waits=[(s_pe, vg), qst_free[qs]],
                                              scale=QSCALE)
                                    mmfree[id(bank)] = (s_act, v_e)
                                    v_st = dma("scalar",
                                               q_hbm[ht * 128:(ht + 1) * 128,
                                                     tc:tc + 512],
                                               QST[qs], SQT[qs],
                                               waits=[(s_act, v_e)])
                                    qst_free[qs] = v_st
                                    qstore[(ht, tc // 512)] = v_st
                        wpc_free[buf] = (s_pe, v_last)
                for i in range(4):
                    rows_free[i] = (s_pe, v_last)
                abc_free = (s_pe, v_last)
                bbc_free = (s_pe, v_last)

                wvre = wv_d.rearrange("(c p) he -> p c he", p=128)
                wvsl_free = (s_pe, v_last)
                for hs in range(8):
                    v_w = dma("gpsimd",
                              WVSL.rearrange("p (c f) -> p c f", f=256),
                              wvre[:, :, hs * 256:(hs + 1) * 256],
                              SWV, waits=[wvsl_free])
                    vg = None
                    for mt in range(8):
                        gm = half * 8 + mt
                        bank = [P0, P1][mt % 2]
                        for c in range(NCH):
                            w = [ln1_ready[half][c]]
                            if c == 0:
                                w += [v_w, mmfree[id(bank)]]
                            vg = mm(bank[:, 0:256], ln1v(c, mt * 128, 128),
                                    WVSL[:, c * 256:(c + 1) * 256],
                                    c == 0, c == NCH - 1,
                                    waits=w, inc=s_pe if c == NCH - 1 else None)
                        v_e = acp(vsb(gm, hs * 256, 256), bank[:, 0:256],
                                  waits=[(s_pe, vg)])
                        mmfree[id(bank)] = (s_act, v_e)
                    wvsl_free = (s_pe, vg)
                    v_last = vg
                wpc_free = [(s_pe, v_last), (s_pe, v_last)]
                qkv_done[half] = v_last

        phase1_pe = (s_pe, s_pe.n)
        phase1_act = (s_act, s_act.n)

        if "ksb" in dump_d:
            for h in range(H):
                dma("sync", dump_d["ksb"][h * 128:(h + 1) * 128, :], ksb(h),
                    SDBG, waits=[phase1_act, (SDBG, SDBG.n)])
        if "vsb" in dump_d:
            for mt in range(16):
                dma("sync", dump_d["vsb"][mt * 128:(mt + 1) * 128, :], vsb(mt),
                    SDBG, waits=[phase1_act, (SDBG, SDBG.n)])
        if "attn" in phases:
            q_free = [phase1_pe] * 4
            pt_free = [phase1_pe] * 6
            z_reg_free = phase1_pe
            recip_free = (SBO[0], SBO[0].n) if SBO[0].n else abc_free
            lns_free = (SBO[1], SBO[1].n) if SBO[1].n else None
            x2st_free = [xs_free[2], xs_free[3], xs_free[6], xs_free[7]]
            wob_free = [wpc_free[0], wpc_free[1]]
            sb_free = {id(P0): mmfree[id(P0)], id(P1): mmfree[id(P1)],
                       id(P7): None}
            sum_free = {id(P4): p4_free, id(P5): p5_free}
            zb_free = {id(P2): p2_free, id(P3): None}
            wore = wo_bf.rearrange("(hc p) d -> p hc d", p=128)
            wob_ready = wcv_state["out"]
            kready = phase1_act
            SCB = [P0, P1, P7]

            for qb in range(4):
                nmt = 4 * qb + 4
                qcols = qb * 512
                sstep = 0
                ob_free = [sb_free[id(P0)], sb_free[id(P1)],
                           zb_free[id(P3)], sb_free[id(P7)]]
                qpend = {}

                def qload(g):
                    for idx, h in enumerate([2 * g, 2 * g + 1]):
                        sl = (2 * g + idx) % 4
                        qpend[h] = (sl, dma("sync", QSL[sl],
                                    q_hbm[h * 128:(h + 1) * 128,
                                          qcols:qcols + 512],
                                    SQL[sl], waits=[q_free[sl], qstore[(h, qb)]]))

                qload(0)
                for g in range(H // 2):
                    if g + 1 < H // 2:
                        qload(g + 1)
                    hh = [2 * g, 2 * g + 1]
                    hst = {}
                    for idx, h in enumerate(hh):
                        sl, v_q = qpend.pop(h)
                        hst[h] = dict(q=v_q, qsl=sl,
                                      zb=[P3, P2][idx], sumb=[P4, P5][idx],
                                      pts=[PT[3 * idx + i] for i in range(3)],
                                      ptbase=3 * idx,
                                      zg=None, sg=None, ready={})

                    def emit_S(h, mt):
                        nonlocal sstep
                        s_ = hst[h]
                        off = mt - 4 * qb
                        qlo = max(0, off * 128)
                        sbank = SCB[sstep % 3]
                        sstep += 1
                        diag = off >= 0
                        v_s = mm(sbank[:, qlo:512], ksb(h, mt * 128, 128),
                                 QSL[s_["qsl"]][:, qlo:512], True, not diag,
                                 waits=[s_["q"], kready, sb_free[id(sbank)]],
                                 inc=None if diag else s_pe)
                        if diag:
                            v_s = mm(sbank[:, qlo:qlo + 128], ident, trineg,
                                     False, True, inc=s_pe)
                        ptb = s_["ptbase"] + mt % 3
                        v_e = aact(PT[ptb][:, qlo:512], sbank[:, qlo:512],
                                   AF.Exp, waits=[(s_pe, v_s), pt_free[ptb]])
                        sb_free[id(sbank)] = (s_act, v_e)
                        s_["ready"][mt] = (qlo, ptb, (s_act, v_e), v_s)

                    def emit_PV(h, mt):
                        s_ = hst[h]
                        qlo, ptb, ptr, v_s = s_["ready"].pop(mt)
                        zbank = s_["zb"]
                        sumb = s_["sumb"]
                        wz = [ptr]
                        if mt == 0:
                            wz.append(zb_free[id(zbank)])
                        s_["zg"] = mm(zbank[:, qlo:512], vsb(mt, h * 128, 128),
                                      PT[ptb][:, qlo:512], mt == 0, mt == nmt - 1,
                                      waits=wz, inc=s_pe if mt == nmt - 1 else None)
                        ws = []
                        if mt == 0:
                            ws.append(sum_free[id(sumb)])
                        s_["sg"] = mm(sumb[:, qlo:512], ones128,
                                      PT[ptb][:, qlo:512], mt == 0, mt == nmt - 1,
                                      waits=ws, inc=s_pe)
                        pt_free[ptb] = (s_pe, s_["sg"])

                    emit_S(hh[0], 0)
                    emit_S(hh[1], 0)
                    for mt in range(nmt):
                        if mt + 1 < nmt:
                            emit_S(hh[0], mt + 1)
                        emit_PV(hh[0], mt)
                        if mt + 1 < nmt:
                            emit_S(hh[1], mt + 1)
                        emit_PV(hh[1], mt)

                    for idx, h in enumerate(hh):
                        s_ = hst[h]
                        zbank = s_["zb"]
                        sumb = s_["sumb"]
                        q_free[s_["qsl"]] = (s_pe, s_["sg"])
                        if qb < 2:
                            v_l = aact(LNS, sumb[:, :], AF.Ln,
                                       waits=[(s_pe, s_["sg"]), lns_free])
                            sum_free[id(sumb)] = (s_act, v_l)
                            v_r = aact(ABC, LNS, AF.Exp, scale=-1.0,
                                       waits=[recip_free, (s_act, v_l)])
                            lns_free = (s_act, v_r)
                            rwait = (s_act, v_r)
                        else:
                            v_r = vrecip(ABC, sumb[:, :],
                                         waits=[(s_pe, s_["sg"]), recip_free])
                            sum_free[id(sumb)] = (s_dve, v_r)
                            rwait = (s_dve, v_r)
                        v_zn = vtt(zv(h), zbank[:, :], ABC, ALU.mult,
                                   waits=[(s_pe, s_["zg"]), rwait,
                                          z_reg_free])
                        recip_free = (s_dve, v_zn)
                        zb_free[id(zbank)] = (s_dve, v_zn)
                        z_ready_last = v_zn
                v_og = None
                for dt in range(16):
                    wb_ = dt % 2
                    v_w = dma("gpsimd",
                              WPC[wb_].rearrange("p (c f) -> p c f", f=128),
                              wore[:, :, dt * 128:(dt + 1) * 128],
                              SWP[wb_], waits=[wob_free[wb_], wob_ready])
                    xsl = [0, 1, 4, 5][dt % 4]
                    v_x = dma("sync", XS[xsl],
                              x_d[dt * 128:(dt + 1) * 128, qcols:qcols + 512],
                              SX[xsl], waits=[xs_free[xsl], xs_free2[xsl]])
                    obank = [P0, P1, P3, P7][dt % 4]
                    vg = None
                    for hc in range(H):
                        w = []
                        if hc == 0:
                            w = [v_w, (s_dve, z_ready_last),
                                 ob_free[dt % 4]]
                        vg = mm(obank[:, :], WPC[wb_][:, hc * 128:(hc + 1) * 128],
                                zv(hc), hc == 0, hc == H - 1,
                                waits=w, inc=s_pe if hc == H - 1 else None)
                    v_og = vg
                    wob_free[wb_] = (s_pe, vg)
                    st = [2, 3, 6, 7][dt % 4]
                    v_ev = vtt(XS[st], obank[:, :], XS[xsl], ALU.add,
                               waits=[(s_pe, vg), v_x, x2st_free[dt % 4]])
                    ob_free[dt % 4] = (s_dve, v_ev)
                    xs_free[xsl] = (s_dve, v_ev)
                    v_st = dma("scalar",
                               x2_hbm[dt * 128:(dt + 1) * 128, qcols:qcols + 512],
                               XS[st], SX[st], waits=[(s_dve, v_ev)])
                    x2st_free[dt % 4] = v_st
                    x2store[(dt, qb)] = v_st
                sb_free[id(P0)] = ob_free[0]
                sb_free[id(P1)] = ob_free[1]
                zb_free[id(P3)] = ob_free[2]
                sb_free[id(P7)] = ob_free[3]
                z_reg_free = (s_pe, v_og)
            mmfree[id(P0)] = sb_free[id(P0)]
            mmfree[id(P1)] = sb_free[id(P1)]
            p2_free = zb_free[id(P2)]
            p4_free = sum_free[id(P4)]
            p5_free = sum_free[id(P5)]
            abc_free = recip_free
            xs_free[2] = x2st_free[0]
            xs_free[3] = x2st_free[1]
            xs_free[6] = x2st_free[2]
            xs_free[7] = x2st_free[3]
            wpc_free = wob_free

        attn_pe = (s_pe, s_pe.n)
        attn_dve = (s_dve, s_dve.n)

        if "x2" in dump_d and "attn" in phases:
            wlast = [x2store[(dt, qb)] for dt in range(NCH) for qb in range(4)]
            wl = {}
            for s, v in wlast:
                wl[s.name] = (s, max(v, wl.get(s.name, (s, 0))[1]))
            wlast = list(wl.values())
            for c in range(NCH):
                for qb in range(4):
                    v_l = dma("sync", XS[0],
                              x2_hbm[c * 128:(c + 1) * 128, qb * 512:(qb + 1) * 512],
                              SX[0], waits=wlast + [xs_free[0]])
                    v_s = dma("sync",
                              dump_d["x2"][c * 128:(c + 1) * 128, qb * 512:(qb + 1) * 512],
                              XS[0], SX[0], waits=[v_l])
                    xs_free[0] = v_s

        if "ffn" in phases:
            wa_re = wa_d.rearrange("(c p) ar -> p c ar", p=128)
            wb_re = wb_d.rearrange("(cc p) j -> p cc j", p=128)
            ww_re = ww_d.rearrange("(c p) j -> p c j", p=128)
            wbar = [attn_pe, wpc_free[0], wpc_free[1]]
            a12_free = [attn_pe, attn_pe]
            DBANKS = [P0, P1, P3, P7]
            rexp_free = [attn_pe, attn_pe]
            vec_free = [attn_pe] * 8
            ln2_free = attn_pe
            a5_free = attn_dve
            a6_free = attn_pe
            a7_free = attn_pe
            a8_free = attn_pe
            p7_free = None
            topk_free = None
            last_out = None

            stat_state = {}

            def emit_stats_chunk(tb, c):
                nonlocal p4_free, p5_free
                tc = tb * 512
                sl = 4 + c % 4
                ssl = c % 2
                v_ld = dma("sync", XS[sl],
                           x2_hbm[c * 128:(c + 1) * 128, tc:tc + 512],
                           SX[sl], waits=[xs_free[sl], xs_free2[sl], x2store[(c, tb)]])
                v_s2 = aact(XS[ssl], XS[sl], AF.Square,
                            waits=[v_ld, xs_free[ssl]])
                v_sum = mm(P4[:, :], ones128f, XS[sl], c == 0, c == NCH - 1,
                           waits=[v_ld] + ([p4_free, consts_ready] if c == 0 else []),
                           inc=s_pe)
                xs_free[sl] = (s_pe, v_sum)
                xs_free2[sl] = (s_act, v_s2)
                v_sq = mm(P5[:, :], ones128f, XS[ssl], c == 0, c == NCH - 1,
                          waits=[(s_act, v_s2)] + ([p5_free] if c == 0 else []),
                          inc=s_pe)
                xs_free[ssl] = (s_pe, v_sq)
                stat_state[tb] = (v_sum, v_sq)

            v_wa = dma("gpsimd", BIG3[:, 0:16384].rearrange("p (c f) -> p c f", f=1024),
                       wa_re, SWA, waits=wbar)
            v_wb = dma("gpsimd", BIG3[:, 16384:32768].rearrange("p (c f) -> p c f", f=2048),
                       wb_re, SWB, waits=wbar)
            for k in range(4):
                dma("gpsimd",
                    BIG2[:, k * 8192:(k + 1) * 8192].rearrange("p (c f) -> p c f", f=2048),
                    ww_re[:, 4 * k:4 * (k + 1), :], SWW, waits=wbar)
            v_ww = (SWW, SWW.n)

            dbank_free = [mmfree[id(P0)], mmfree[id(P1)],
                          zb_free[id(P3)] if "attn" in phases else None,
                          sb_free[id(P7)] if "attn" in phases else None]
            for tb in range(4):
                tc = tb * 512
                for c in range(NCH):
                    emit_stats_chunk(tb, c)
                v_abc, v_bbc = lnrows(*stat_state[tb])

                last_apply = None
                apply_chunk_v = [None] * NCH
                for c in range(NCH):
                    sl = 4 + c % 4
                    v_ld = dma("sync", XS[sl],
                               x2_hbm[c * 128:(c + 1) * 128, tc:tc + 512],
                               SX[sl], waits=[xs_free[sl], xs_free2[sl]])
                    scr = 2 + c % 2
                    v_t1 = vtt(XS[scr], XS[sl], RWF[2], ALU.mult,
                               waits=[v_ld, v_abc, xs_free[scr]])
                    xs_free[sl] = (s_dve, v_t1)
                    v_t2 = vtt(ln2v(c), XS[scr], RWF[3], ALU.subtract,
                               waits=[v_bbc, ln2_free, (s_dve, v_t1)])
                    xs_free[scr] = (s_dve, v_t2)
                    last_apply = v_t2
                    apply_chunk_v[c] = (s_dve, v_t2)
                ln2_ready = (s_dve, last_apply)
                abc_free = ln2_ready
                bbc_free = ln2_ready
                ln2_rdy_c = apply_chunk_v

                v_arr_g = None
                for at in range(8):
                    bank = [P0, P1][at % 2]
                    vg = None
                    for c in range(NCH):
                        w = [ln2_rdy_c[c]]
                        if c == 0:
                            w += [v_wa, mmfree[id(bank)]]
                        vg = mm(bank[:, :], wasb(c)[:, at * 128:(at + 1) * 128],
                                ln2v(c), c == 0, c == NCH - 1,
                                waits=w, inc=s_pe if c == NCH - 1 else None)
                    v_ev = acp(vecv(at), bank[:, :], waits=[(s_pe, vg), vec_free[at]])
                    v_s2 = aact([A1, A2][at % 2], bank[:, :], AF.Square,
                                waits=[a12_free[at % 2]])
                    mmfree[id(bank)] = (s_act, v_s2)
                    vec_free[at] = (s_act, v_ev)
                    v_arr_g = mm(P4[0:64, :], selA[at], [A1, A2][at % 2],
                                 at == 0, at == 7,
                                 waits=[(s_act, v_s2)] + ([p4_free] if at == 0 else []),
                                 inc=s_pe)
                    a12_free[at % 2] = (s_pe, v_arr_g)
                vecs_ready = {at: vec_free[at] for at in range(8)}

                v_ae = acp(XS[0][0:64, :], P4[0:64, :],
                           waits=[(s_pe, v_arr_g), xs_free[0]])
                p4_free = (s_act, v_ae)
                v_tes = []
                v_tr = None
                for sl4 in range(4):
                    v_tr = tr(P2[:, 0:64], XS[0][0:64, sl4 * 128:(sl4 + 1) * 128],
                              identf64, waits=[(s_act, v_ae), p2_free])
                    v_te = acp(XS[1][:, sl4 * 64:(sl4 + 1) * 64], P2[:, 0:64],
                               waits=[(s_pe, v_tr)] + ([xs_free[1]] if sl4 == 0 else []))
                    p2_free = (s_act, v_te)
                    v_tes.append(v_te)
                xs_free[0] = (s_pe, v_tr)

                wpart = {}
                for jt in range(4):
                    bank = DBANKS[jt % 4]
                    vg = None
                    for c in range(NCH):
                        w = [ln2_rdy_c[c]]
                        if c == 0:
                            w += [v_ww, dbank_free[jt % 4]]
                        vg = mm(bank[:, :], wwsb(c)[:, jt * 128:(jt + 1) * 128],
                                ln2v(c), c == 0, False, waits=w)
                    wpart[jt] = True


                MSL = [TOPK[:, 576 + i:577 + i] for i in range(4)]
                INDT = TOPK[:, 512:576]
                RTT = TOPK[:, 640:704]
                NX = [[TOPK[:, s * 128:s * 128 + 64],
                       TOPK[:, s * 128 + 64:s * 128 + 128]] for s in range(4)]
                cur = {}
                vprev = None
                for s in range(4):
                    x0 = XS[1][:, s * 64:(s + 1) * 64]
                    vprev = vmax(MSL[s], x0, waits=[(s_act, v_tes[s])] +
                                 ([topk_free] if s == 0 else []))
                    cur[s] = x0
                for k in range(3):
                    for s in range(4):
                        vprev = vts(INDT, cur[s], MSL[s], ALU.is_ge,
                                    waits=[(s_dve, vprev)])
                        nxt = NX[s][k % 2]
                        vprev = vstt(nxt, INDT, -1e9, cur[s], ALU.mult, ALU.add,
                                     waits=[(s_dve, vprev)])
                        vprev = vmax(MSL[s], nxt, waits=[(s_dve, vprev)])
                        cur[s] = nxt
                rt_free = None
                v_rfm = None
                v_rt_last = None
                for s in range(4):
                    x0 = XS[1][:, s * 64:(s + 1) * 64]
                    v_r01 = vts(RTT, x0, MSL[s], ALU.is_ge,
                                waits=[(s_dve, vprev)] + ([rt_free] if rt_free else []))
                    vprev = v_r01
                    v_rt_last = v_r01
                    v_tr2 = tr(P2[0:64, 0:128], RTT, identf128,
                               waits=[(s_dve, v_r01), p2_free])
                    rt_free = (s_pe, v_tr2)
                    v_rfm = acp(A6[0:64, s * 128:(s + 1) * 128],
                                P2[0:64, 0:128],
                                waits=[(s_pe, v_tr2)] + ([a6_free] if s == 0 else []))
                    p2_free = (s_act, v_rfm)
                topk_free = rt_free
                xs_free[1] = (s_dve, v_rt_last)

                if "arr" in dump_d:
                    v_da = dma("sync", dump_d["arr"][:, tc:tc + 512],
                               XS[0][0:64, :], SDBG,
                               waits=[(s_act, v_ae), (SDBG, SDBG.n)])
                    xs_free[0] = v_da
                    v_dt = dma("sync", dump_d["atm"][:, tb * 256:(tb + 1) * 256],
                               XS[1][:, 0:256], SDBG,
                               waits=[(s_act, v_tes[-1]), (SDBG, SDBG.n)])
                    v_dr = dma("sync", dump_d["routes"][:, tc:tc + 512],
                               A6[0:64, :], SDBG,
                               waits=[(s_act, v_rfm), (SDBG, SDBG.n)])
                    a6_free = v_dr

                v01g = None
                for cc in range(8):
                    v_re = mm(P6[:, :], selExp[cc], A6[0:64, :], True, True,
                              waits=[(s_act, v_rfm), p6_free], inc=s_pe)
                    v_ree = acp([A3, A4][cc % 2], P6[:, :],
                                waits=[(s_pe, v_re), rexp_free[cc % 2]])
                    p6_free = (s_act, v_ree)
                    v_mk = vtt([A1, A2][cc % 2], vecv(cc), [A3, A4][cc % 2], ALU.mult,
                               waits=[(s_act, v_ree), vecs_ready[cc], a12_free[cc % 2]])
                    rexp_free[cc % 2] = (s_dve, v_mk)
                    v01g = mm(P5[0:16, :], selR, [A1, A2][cc % 2], cc == 0, cc == 7,
                              waits=[(s_dve, v_mk)] + ([p5_free] if cc == 0 else []),
                              inc=s_pe)
                    a12_free[cc % 2] = (s_pe, v01g)

                v_vq = acp(A8[0:16, :], P5[0:16, :],
                           waits=[(s_pe, v01g), a8_free], scale=0.0625)
                p5_free = (s_act, v_vq)
                v_vx = mm(P6[:, :], selTile, A8[0:16, :], True, True,
                          waits=[(s_act, v_vq), p6_free], inc=s_pe)
                a8_free = (s_pe, v_vx)
                v_ve = acp(A5, P6[:, :], waits=[(s_pe, v_vx), a5_free])
                p6_free = (s_act, v_ve)

                u_ready = None
                v_re = None
                for cc in range(8):
                    v_re = mm(P6[:, :], selExp[cc], A6[0:64, :], True, True,
                              waits=[p6_free], inc=s_pe)
                    v_ree = acp([A3, A4][cc % 2], P6[:, :],
                                waits=[(s_pe, v_re), rexp_free[cc % 2]])
                    p6_free = (s_act, v_ree)
                    v_u = vtt(vecv(cc), [A3, A4][cc % 2], A5, ALU.mult,
                              waits=[(s_act, v_ree), (s_act, v_ve), vec_free[cc]])
                    rexp_free[cc % 2] = (s_dve, v_u)
                    vec_free[cc] = (s_dve, v_u)
                    u_ready = v_u
                a6_free = (s_pe, v_re)
                a5_free = (s_dve, u_ready)
                if "u" in dump_d:
                    for cc in range(8):
                        v_du = dma("sync",
                                   dump_d["u"][cc * 128:(cc + 1) * 128, tc:tc + 512],
                                   vecv(cc), SDBG,
                                   waits=[(s_dve, u_ready), (SDBG, SDBG.n)])
                        vec_free[cc] = v_du

                for jt in range(16):
                    bank = DBANKS[jt % 4]
                    xsl = [0, 1, 4, 5][jt % 4]
                    v_x2 = dma("sync", XS[xsl],
                               x2_hbm[jt * 128:(jt + 1) * 128, tc:tc + 512],
                               SX[xsl], waits=[xs_free[xsl], xs_free2[xsl],
                                               x2store[(jt, tb)]])
                    if jt not in wpart:
                        vg = None
                        for c in range(NCH):
                            w = []
                            if c == 0:
                                w = [dbank_free[jt % 4]]
                            vg = mm(bank[:, :], wwsb(c)[:, jt * 128:(jt + 1) * 128],
                                    ln2v(c), c == 0, False, waits=w)
                    vg = None
                    for cc in range(8):
                        w = []
                        if cc == 0:
                            w = [(s_dve, u_ready), v_wb]
                        vg = mm(bank[:, :], wbsb(cc)[:, jt * 128:(jt + 1) * 128],
                                vecv(cc), False, cc == 7, waits=w,
                                inc=s_pe if cc == 7 else None)
                    st = [2, 3, 6, 7][jt % 4]
                    v_o = vstt(XS[st], bank[:, :], BIASR[:, jt:jt + 1], XS[xsl],
                               ALU.add, ALU.add,
                               waits=[(s_pe, vg), v_x2, xs_free[st]])
                    dbank_free[jt % 4] = (s_dve, v_o)
                    xs_free[xsl] = (s_dve, v_o)
                    v_st = dma("scalar", out_d[jt * 128:(jt + 1) * 128, tc:tc + 512],
                               XS[st], SX[st], waits=[(s_dve, v_o)])
                    xs_free[st] = v_st
                    last_out = v_st
                mmfree[id(P0)] = dbank_free[0]
                mmfree[id(P1)] = dbank_free[1]

                ln2_free = (s_pe, s_pe.n)
                for cc in range(8):
                    vec_free[cc] = ln2_free



            final_wait = ("sync", [(SX[2], SX[2].n), (SX[3], SX[3].n)])
        else:
            final_wait = ("sync", [(s, s.n) for s in sems if s.n > 0
                                   and s not in (s_pe, s_act, s_dve)])

        LAST_PLAN = (p, final_wait)
        with nc.Block() as block:
            def mkrunner(engname):
                def r(eng):
                    waited = {}
                    for fn, waits, inc, n in p.q[engname]:
                        for s, v in waits:
                            if waited.get(s.name, 0) < v:
                                eng.wait_ge(s.h, v)
                                waited[s.name] = v
                        inst = fn(eng)
                        if inc is not None:
                            inst.then_inc(inc.h, n)
                    if engname == final_wait[0]:
                        for fs, fv in final_wait[1]:
                            if fv > 0:
                                eng.wait_ge(fs.h, fv)
                return r

            block.sync(mkrunner("sync"))
            block.tensor(mkrunner("tensor"))
            block.vector(mkrunner("vector"))
            block.scalar(mkrunner("scalar"))
            block.gpsimd(mkrunner("gpsimd"))

    return nc


def _consts():
    cb = np.zeros((128, 2304), dtype=np.float32)
    cb[:, 0:128] = np.eye(128, dtype=np.float32)
    for m in range(128):
        cb[m, 128 + m:256] = 1.0
    for c in range(8):
        for pp in range(128):
            cb[pp, 256 + c * 64 + 8 * c + pp // 16] = 1.0
    for c in range(8):
        for i in range(128):
            a = 8 * c + i // 16
            cb[a, 768 + c * 128 + i] = 1.0
    for pp in range(128):
        cb[pp, 1792 + pp % 16] = 1.0
    for i in range(128):
        cb[i % 16, 1808 + i] = 1.0
    cb[:, 1936] = 1.0
    for m in range(128):
        cb[m, 2048:2048 + m] = -1e9
    cb[:, 2176:2304] = 1.0
    cf = np.zeros((128, 512), dtype=np.float32)
    cf[0, 0:128] = 1.0
    cf[0:64, 128:192] = np.eye(64, dtype=np.float32)
    cf[:, 192] = EPS
    cf[:, 256:384] = np.eye(128, dtype=np.float32)
    cf[:, 384:512] = 1.0
    return cb.astype(ml_dtypes.bfloat16), cf


def make_in_maps(L0_attn_x, L0_attn_W_Q, L0_attn_W_K, L0_attn_W_V, L0_attn_W_O,
                 L0_ffn_A, L0_ffn_B, L0_ffn_W, L0_ffn_bias):
    x = np.asarray(L0_attn_x, dtype=np.float32)
    wq = np.ascontiguousarray(np.asarray(L0_attn_W_Q, dtype=np.float32).reshape(D, D).T)
    wk = np.ascontiguousarray(np.asarray(L0_attn_W_K, dtype=np.float32).reshape(D, D).T)
    wv = np.ascontiguousarray(np.asarray(L0_attn_W_V, dtype=np.float32).reshape(D, D).T)
    wo = np.ascontiguousarray(
        np.asarray(L0_attn_W_O, dtype=np.float32).transpose(0, 2, 1).reshape(D, D))
    wa = np.ascontiguousarray(np.asarray(L0_ffn_A, dtype=np.float32).reshape(AR, D).T)
    wb = np.ascontiguousarray(
        np.asarray(L0_ffn_B, dtype=np.float32).transpose(0, 2, 1).reshape(AR, D))
    ww = np.ascontiguousarray(np.asarray(L0_ffn_W, dtype=np.float32).T)
    biasr = np.ascontiguousarray(
        np.asarray(L0_ffn_bias, dtype=np.float32).reshape(16, 128).T)
    cb, cf = _consts()
    shared = dict(wq=wq, wk=wk, wv=wv, wo=wo, wa=wa, wb=wb, ww=ww,
                  biasr=biasr, constb=cb, constf=cf)
    maps = []
    for b in range(8):
        m = dict(shared)
        m["x"] = np.ascontiguousarray(x[b].T)
        maps.append(m)
    return maps


def kernel(L0_attn_x, L0_attn_W_Q, L0_attn_W_K, L0_attn_W_V, L0_attn_W_O,
           L0_ffn_A, L0_ffn_B, L0_ffn_W, L0_ffn_bias):
    maps = make_in_maps(L0_attn_x, L0_attn_W_Q, L0_attn_W_K, L0_attn_W_V,
                        L0_attn_W_O, L0_ffn_A, L0_ffn_B, L0_ffn_W, L0_ffn_bias)
    nc = build_nc()
    res = run_bass_kernel_spmd(nc, maps, core_ids=list(range(8)))
    out = np.stack([np.ascontiguousarray(np.asarray(r["out"], dtype=np.float32).T)
                    for r in res.results])
    return out


# version 19 (final)
# speedup vs baseline: 1.1571x; 1.1571x over previous
import sys
sys.path.insert(0, "/opt/trn_rl_repo")
import numpy as np
import ml_dtypes
import concourse.bass as bass
import concourse.mybir as mybir
from concourse.bass_utils import run_bass_kernel_spmd

F32 = mybir.dt.float32
BF16 = mybir.dt.bfloat16
AF = mybir.ActivationFunctionType
ALU = mybir.AluOpType
AX = mybir.AxisListType

D = 2048
T = 2048
H = 16
NCH = 16
AR = 1024
QSCALE = 1.0 / float(np.sqrt(128.0))
EPS = 1e-5

ENGS = ("sync", "tensor", "vector", "scalar", "gpsimd")


class Sem:
    __slots__ = ("name", "h", "n")

    def __init__(self, name):
        self.name = name
        self.h = None
        self.n = 0


class Plan:
    def __init__(self):
        self.q = {e: [] for e in ENGS}

    def op(self, eng, fn, waits=(), inc=None, n=1):
        w = []
        for t in waits:
            if t is None:
                continue
            s, v = t
            if s is not None and v is not None and v > 0:
                w.append((s, v))
        self.q[eng].append((fn, w, inc, n))
        if inc is not None:
            inc.n += n
            return inc.n
        return None


LAST_PLAN = None


def build_nc(phases=("p1", "attn", "ffn"), dump=()):
    global LAST_PLAN
    nc = bass.Bass("TRN2", target_bir_lowering=False)

    x_d = nc.dram_tensor("x", [D, T], F32, kind="ExternalInput").ap()
    wq_d = nc.dram_tensor("wq", [D, D], F32, kind="ExternalInput").ap()
    wk_d = nc.dram_tensor("wk", [D, D], F32, kind="ExternalInput").ap()
    wv_d = nc.dram_tensor("wv", [D, D], F32, kind="ExternalInput").ap()
    wo_d = nc.dram_tensor("wo", [D, D], F32, kind="ExternalInput").ap()
    wa_d = nc.dram_tensor("wa", [D, AR], F32, kind="ExternalInput").ap()
    wb_d = nc.dram_tensor("wb", [AR, D], F32, kind="ExternalInput").ap()
    ww_d = nc.dram_tensor("ww", [D, D], F32, kind="ExternalInput").ap()
    bias_d = nc.dram_tensor("biasr", [128, 16], F32, kind="ExternalInput").ap()
    constb_d = nc.dram_tensor("constb", [128, 2304], BF16, kind="ExternalInput").ap()
    constf_d = nc.dram_tensor("constf", [128, 512], F32, kind="ExternalInput").ap()

    out_d = nc.dram_tensor("out", [D, T], F32, kind="ExternalOutput").ap()
    q_hbm = nc.dram_tensor("q_scr", [D, T], BF16).ap()
    wo_bf = nc.dram_tensor("wo_bf_scr", [D, D], BF16).ap()
    x2_hbm = nc.dram_tensor("x2_scr", [D, T], F32).ap()

    dump_d = {}
    for name in dump:
        if name == "ffn1":
            dump_d["arr"] = nc.dram_tensor("dump_arr", [64, T], F32,
                                           kind="ExternalOutput").ap()
            dump_d["routes"] = nc.dram_tensor("dump_routes", [64, T], BF16,
                                              kind="ExternalOutput").ap()
            dump_d["u"] = nc.dram_tensor("dump_u", [AR, T], BF16,
                                         kind="ExternalOutput").ap()
            dump_d["atm"] = nc.dram_tensor("dump_atm", [128, 1024], F32,
                                           kind="ExternalOutput").ap()
            continue
        if name == "lnd":
            dump_d["abc"] = nc.dram_tensor("dump_abc", [128, 4096], F32,
                                           kind="ExternalOutput").ap()
            dump_d["lnr"] = nc.dram_tensor("dump_lnr", [2, 2048], F32,
                                           kind="ExternalOutput").ap()
            continue
        dt = F32 if name in ("x2",) else BF16
        dump_d[name] = nc.dram_tensor("dump_" + name, [D, T], dt,
                                      kind="ExternalOutput").ap()

    p = Plan()
    s_pe = Sem("s_pe")
    s_act = Sem("s_act")
    s_dve = Sem("s_dve")
    SC = Sem("sc")
    SX = [Sem(f"sx{i}") for i in range(8)]
    SQT = [Sem(f"sqt{i}") for i in range(2)]
    SQL = [Sem(f"sql{i}") for i in range(4)]
    SWP = [Sem(f"swp{i}") for i in range(2)]
    SWV = Sem("swv")
    SWA = Sem("swa")
    SWB = Sem("swb")
    SWW = Sem("sww")
    SDBG = Sem("sdbg")
    SBI = Sem("sbi")
    SBO = [Sem("sbo0"), Sem("sbo1")]
    sems = ([s_pe, s_act, s_dve, SC, SWV, SWA, SWB, SWW, SDBG, SBI]
            + SBO + SX + SQT + SQL + SWP)

    import contextlib
    stack = contextlib.ExitStack()

    def sb(name, shape, dt):
        return stack.enter_context(nc.sbuf_tensor(name, shape, dt))

    def psum(name, shape, dt):
        return stack.enter_context(nc.psum_tensor(name, shape, dt))

    with stack:
        BIG1 = sb("big1", [128, 16384], BF16)
        BIG2 = sb("big2", [128, 32768], BF16)
        BIG3 = sb("big3", [128, 32768], BF16)
        WST = sb("wst", [128, 5120], BF16)
        XB = sb("xb", [128, 4096], F32)
        SMALL = sb("small", [128, 1024], F32)
        ROWS = sb("rows", [128, 2048], F32)
        TOPK = sb("topk", [128, 768], F32)
        BIASR = sb("biasr_sb", [128, 16], F32)
        CONSTB = sb("constb_sb", [128, 2304], BF16)
        CONSTF = sb("constf_sb", [128, 512], F32)

        def ln1v(c, lo, n):
            return BIG1[:, c * 1024 + lo: c * 1024 + lo + n]

        def zv(hc):
            return BIG1[:, hc * 512: (hc + 1) * 512]

        QSL = [BIG1[:, 8192:8704], BIG1[:, 8704:9216],
               BIG1[:, 12288:12800], BIG1[:, 12800:13312]]
        PT = [BIG1[:, 9216 + i * 512: 9216 + (i + 1) * 512] for i in range(6)]

        def ln2v(c):
            return BIG1[:, c * 512: (c + 1) * 512]

        def vecv(at):
            return BIG1[:, 8192 + at * 512: 8192 + (at + 1) * 512]

        A1 = BIG1[:, 12288:12800]
        A2 = BIG1[:, 12800:13312]
        A3 = BIG1[:, 13312:13824]
        A4 = BIG1[:, 13824:14336]
        A5 = BIG1[:, 14336:14848]
        A6 = BIG1[:, 14848:15360]
        A7 = BIG1[:, 15360:15616]
        A8 = BIG1[:, 15616:16128]

        def ksb(h, lo=0, n=2048):
            return BIG2[:, h * 2048 + lo: h * 2048 + lo + n]

        def wwsb(c):
            return BIG2[:, c * 2048: (c + 1) * 2048]

        def vsb(mt, lo=0, n=2048):
            return BIG3[:, mt * 2048 + lo: mt * 2048 + lo + n]

        def wasb(c):
            return BIG3[:, c * 1024: (c + 1) * 1024]

        def wbsb(cc):
            return BIG3[:, 16384 + cc * 2048: 16384 + (cc + 1) * 2048]

        WPC = [WST[:, 0:2048], WST[:, 2048:4096]]
        WVSL = WST[:, 0:4096]
        QST = [WST[:, 4096 + i * 512: 4096 + (i + 1) * 512] for i in range(2)]

        XS = [XB[:, i * 512: (i + 1) * 512] for i in range(8)]
        ABC = SMALL[:, 0:512]
        LNS = SMALL[:, 512:1024]
        RW = [ROWS[0:1, i * 512:(i + 1) * 512] for i in range(4)]
        RWF = [ROWS[:, i * 512:(i + 1) * 512] for i in range(4)]

        ident = CONSTB[:, 0:128]
        tri = CONSTB[:, 128:256]
        selA = [CONSTB[:, 256 + c * 64: 256 + (c + 1) * 64] for c in range(8)]
        selExp = [CONSTB[0:64, 768 + c * 128: 768 + (c + 1) * 128] for c in range(8)]
        selR = CONSTB[:, 1792:1808]
        selTile = CONSTB[0:16, 1808:1936]
        ones_col = CONSTB[:, 1936:1937]
        trineg = CONSTB[:, 2048:2176]
        ones128 = CONSTB[:, 2176:2304]
        onesf = CONSTF[0:1, 0:128]
        identf64 = CONSTF[0:64, 128:192]
        epsap = CONSTF[:, 192:193]
        identf128 = CONSTF[:, 256:384]
        ones128f = CONSTF[:, 384:512]

        P0 = psum("p0", [128, 512], F32)
        P1 = psum("p1", [128, 512], F32)
        P2 = psum("p2", [128, 512], F32)
        P3 = psum("p3", [128, 512], F32)
        P4 = psum("p4", [128, 512], F32)
        P5 = psum("p5", [128, 512], F32)
        P6 = psum("p6", [128, 512], F32)
        P7 = psum("p7", [128, 512], F32)

        for s in sems:
            s.h = stack.enter_context(nc.semaphore(s.name))

        def dma(eng, out, in_, sem, waits=()):
            v = p.op(eng, lambda e, o=out, i=in_: e.dma_start(out=o, in_=i),
                     waits=waits, inc=sem, n=16)
            return (sem, v)

        def mm(out, lhsT, rhs, start, stop, waits=(), inc=None):
            return p.op("tensor",
                        lambda e, o=out, l=lhsT, r=rhs, st=start, sp=stop:
                        e.matmul(o, l, r, start=st, stop=sp),
                        waits=waits, inc=inc)

        def tr(out, in_, idn, waits=()):
            return p.op("tensor",
                        lambda e, o=out, i=in_, dd=idn: e.transpose(o, i, dd),
                        waits=waits, inc=s_pe)

        def acp(out, in_, waits=(), scale=None):
            if scale is None:
                return p.op("scalar", lambda e, o=out, i=in_: e.copy(o, i),
                            waits=waits, inc=s_act)
            return p.op("scalar",
                        lambda e, o=out, i=in_, sc=scale:
                        e.activation(o, i, AF.Identity, scale=sc),
                        waits=waits, inc=s_act)

        def aspace(waits=()):
            return p.op("scalar",
                        lambda e: e.copy(ROWS[0:1, 2040:2048], ROWS[0:1, 2032:2040]),
                        waits=waits, inc=s_act)

        def aact(out, in_, func, waits=(), scale=None, bias=None):
            kw = {}
            if scale is not None:
                kw["scale"] = scale
            if bias is not None:
                kw["bias"] = bias
            return p.op("scalar",
                        lambda e, o=out, i=in_, f=func, k=kw: e.activation(o, i, f, **k),
                        waits=waits, inc=s_act)

        def vtt(out, in0, in1, aop, waits=()):
            return p.op("vector",
                        lambda e, o=out, a=in0, b=in1, op=aop: e.tensor_tensor(o, a, b, op),
                        waits=waits, inc=s_dve)

        def vcp(out, in_, waits=()):
            return p.op("vector",
                        lambda e, o=out, i=in_: e.tensor_copy(o, i),
                        waits=waits, inc=s_dve)

        def vts(out, in0, sc1, aop, waits=()):
            return p.op("vector",
                        lambda e, o=out, a=in0, s1=sc1, op=aop:
                        e.tensor_scalar(o, a, s1, None, op),
                        waits=waits, inc=s_dve)

        def vstt(out, in0, sc, in1, op0, op1, waits=()):
            return p.op("vector",
                        lambda e, o=out, a=in0, s1=sc, b=in1, o0=op0, o1=op1:
                        e.scalar_tensor_tensor(o, a, s1, b, o0, o1),
                        waits=waits, inc=s_dve)

        def vmax(out, in_, waits=()):
            return p.op("vector",
                        lambda e, o=out, i=in_: e.reduce_max(o, i, axis=AX.X),
                        waits=waits, inc=s_dve)

        def vrecip(out, in_, waits=()):
            return p.op("vector",
                        lambda e, o=out, i=in_: e.reciprocal(o, i),
                        waits=waits, inc=s_dve)

        dma("sync", CONSTB[:, :], constb_d, SC)
        dma("sync", CONSTF[:, :], constf_d, SC)
        dma("sync", BIASR[:, :], bias_d, SC)
        consts_ready = (SC, SC.n)

        xs_free = [None] * 8
        xs_free2 = [None] * 8
        wpc_free = [None, None]
        qst_free = [None, None]
        p4_free = None
        p5_free = None
        p6_free = None
        p2_free = None
        mmfree = {id(P0): None, id(P1): None}
        abc_free = None
        bbc_free = None
        rows_free = [None, None, None, None]
        qstore = {}
        x2store = {}

        def lnrows(v_sum, v_sq):
            nonlocal abc_free, bbc_free, p4_free, p5_free
            v_m = acp(RWF[0], P4[:, :], waits=[(s_pe, v_sum), rows_free[0]],
                      scale=1.0 / D)
            v_ms = acp(RWF[1], P5[:, :], waits=[(s_pe, v_sq), rows_free[1]],
                       scale=1.0 / D)
            p4_free = (s_act, v_m)
            p5_free = (s_act, v_ms)
            v_m2 = aact(RWF[2], RWF[0], AF.Square,
                        waits=[(s_act, v_m), rows_free[2], abc_free])
            v_var = vtt(RWF[3], RWF[1], RWF[2], ALU.subtract,
                        waits=[(s_act, v_m2), rows_free[3], bbc_free])
            rows_free[1] = (s_dve, v_var)
            v_lnv = aact(RWF[1], RWF[3], AF.Ln, waits=[(s_dve, v_var)],
                         bias=epsap)
            v_rstd = aact(RWF[2], RWF[1], AF.Exp, waits=[(s_act, v_lnv)],
                          scale=-0.5)
            rows_free[2] = (s_act, v_rstd)
            v_s = vtt(RWF[3], RWF[0], RWF[2], ALU.mult, waits=[(s_act, v_rstd)])
            rows_free[0] = (s_dve, v_s)
            rows_free[3] = (s_dve, v_s)
            return (s_act, v_rstd), (s_dve, v_s)

        ln1_done = [None, None]
        qkv_done = [None, None]

        if "p1" in phases:
            ln1_ready = [[None] * NCH, [None] * NCH]
            wcv_stage = [SMALL.bitcast(BF16)[:, i * 512:(i + 1) * 512] for i in range(2)]
            wcv_state = {"n": 0, "stf": [None, None], "out": None}

            def emit_wo_convert(n):
                for _ in range(n):
                    i = wcv_state["n"]
                    if i >= 64:
                        return
                    cch, hlf = i // 4, i % 4
                    sl = i % 4
                    b = i % 2
                    v_l = dma("sync", XS[sl],
                              wo_d[cch * 128:(cch + 1) * 128,
                                   hlf * 512:(hlf + 1) * 512],
                              SX[sl], waits=[xs_free[sl]])
                    v_cp = vcp(wcv_stage[b], XS[sl],
                               waits=[v_l, wcv_state["stf"][b]])
                    xs_free[sl] = (s_dve, v_cp)
                    v_st = dma("scalar",
                               wo_bf[cch * 128:(cch + 1) * 128,
                                     hlf * 512:(hlf + 1) * 512],
                               wcv_stage[b], SBO[b], waits=[(s_dve, v_cp)])
                    wcv_state["stf"][b] = v_st
                    wcv_state["out"] = v_st
                    wcv_state["n"] = i + 1

            def emit_stats1(half, tb, cs=range(NCH)):
                nonlocal p4_free, p5_free, v_sum1, v_sq1
                t0 = half * 1024
                tc = t0 + tb * 512
                for c in cs:
                    sl = 4 + c % 4
                    ssl = c % 2
                    v_ld = dma("sync", XS[sl],
                               x_d[c * 128:(c + 1) * 128, tc:tc + 512],
                               SX[sl], waits=[xs_free[sl], xs_free2[sl]])
                    v_s2 = aact(XS[ssl], XS[sl], AF.Square,
                                waits=[v_ld, consts_ready, xs_free[ssl]])
                    v_sum1 = mm(P4[:, :], ones128f, XS[sl], c == 0, c == NCH - 1,
                                waits=[v_ld] + ([p4_free, consts_ready] if c == 0 else []),
                                inc=s_pe)
                    xs_free[sl] = (s_pe, v_sum1)
                    xs_free2[sl] = (s_act, v_s2)
                    v_sq1 = mm(P5[:, :], ones128f, XS[ssl], c == 0, c == NCH - 1,
                               waits=[(s_act, v_s2)] + ([p5_free] if c == 0 else []),
                               inc=s_pe)
                    xs_free[ssl] = (s_pe, v_sq1)
                return v_sum1, v_sq1

            v_sum1 = v_sq1 = None
            wob_out = None
            for half in range(2):
                wqb1_virgin = [True]
                t0 = half * 1024
                prev_qkv = qkv_done[1 - half]
                last_apply = None
                for tb in range(2):
                    tc = t0 + tb * 512
                    loc = tb * 512
                    v_sum, v_sq = emit_stats1(half, tb)
                    v_abc, v_bbc = lnrows(v_sum, v_sq)
                    for c in range(NCH):
                        sl = 4 + c % 4
                        v_ld = dma("sync", XS[sl],
                                   x_d[c * 128:(c + 1) * 128, tc:tc + 512],
                                   SX[sl], waits=[xs_free[sl], xs_free2[sl]])
                        scr = 2 + c % 2
                        v_t1 = vtt(XS[scr], XS[sl], RWF[2], ALU.mult,
                                   waits=[v_ld, v_abc, xs_free[scr]])
                        xs_free[sl] = (s_dve, v_t1)
                        wt2 = [v_bbc, (s_dve, v_t1)]
                        if prev_qkv is not None and c == 0:
                            wt2.append((s_pe, prev_qkv))
                        v_t2 = vtt(ln1v(c, loc, 512), XS[scr], RWF[3], ALU.subtract,
                                   waits=wt2)
                        xs_free[scr] = (s_dve, v_t2)
                        last_apply = v_t2
                        ln1_ready[half][c] = (s_dve, v_t2)
                    abc_free = (s_dve, last_apply)
                    bbc_free = (s_dve, last_apply)
                ln1_done[half] = (s_dve, last_apply)
                if "ln1d" in dump_d:
                    for c in range(NCH):
                        dma("sync",
                            dump_d["ln1d"][c * 128:(c + 1) * 128,
                                           half * 1024:(half + 1) * 1024],
                            ln1v(c, 0, 1024), SDBG,
                            waits=[ln1_done[half], (SDBG, SDBG.n)])

                v_last = None
                WQB2 = [WST[:, 0:4096], ROWS.bitcast(BF16)[:, 0:4096]]
                for proj, wd in (("k", wk_d), ("q", wq_d)):
                    wre = wd.rearrange("(c p) he -> p c he", p=128)
                    for pg in range(H // 2):
                        emit_wo_convert(2)
                        buf = pg % 2
                        wal = []
                        if buf == 1 and wqb1_virgin[0]:
                            wal = [ln1_ready[half][NCH - 1]]
                            wqb1_virgin[0] = False
                        v_w = dma("gpsimd",
                                  WQB2[buf].rearrange("p (c f) -> p c f", f=256),
                                  wre[:, :, pg * 256:(pg + 1) * 256],
                                  SWP[buf], waits=[wpc_free[buf]] + wal)
                        for hti in range(2):
                            ht = 2 * pg + hti
                            for ts in range(2):
                                tc = t0 + ts * 512
                                loc = ts * 512
                                bank = [P0, P1][(ht * 2 + ts) % 2]
                                vg = None
                                for c in range(NCH):
                                    w = [ln1_ready[half][c]]
                                    if c == 0:
                                        w += [v_w, mmfree[id(bank)]]
                                    vg = mm(bank[:, :],
                                            WQB2[buf][:, c * 256 + hti * 128:
                                                      c * 256 + hti * 128 + 128],
                                            ln1v(c, loc, 512), c == 0, c == NCH - 1,
                                            waits=w, inc=s_pe if c == NCH - 1 else None)
                                v_last = vg
                                if proj == "k":
                                    v_e = acp(ksb(ht, tc, 512), bank[:, :],
                                              waits=[(s_pe, vg)])
                                    mmfree[id(bank)] = (s_act, v_e)
                                else:
                                    qs = (ht * 2 + ts) % 2
                                    v_e = acp(QST[qs], bank[:, :],
                                              waits=[(s_pe, vg), qst_free[qs]],
                                              scale=QSCALE)
                                    mmfree[id(bank)] = (s_act, v_e)
                                    v_st = dma("scalar",
                                               q_hbm[ht * 128:(ht + 1) * 128,
                                                     tc:tc + 512],
                                               QST[qs], SQT[qs],
                                               waits=[(s_act, v_e)])
                                    qst_free[qs] = v_st
                                    qstore[(ht, tc // 512)] = v_st
                        wpc_free[buf] = (s_pe, v_last)
                for i in range(4):
                    rows_free[i] = (s_pe, v_last)
                abc_free = (s_pe, v_last)
                bbc_free = (s_pe, v_last)

                wvre = wv_d.rearrange("(c p) he -> p c he", p=128)
                wvsl_free = (s_pe, v_last)
                for hs in range(8):
                    v_w = dma("gpsimd",
                              WVSL.rearrange("p (c f) -> p c f", f=256),
                              wvre[:, :, hs * 256:(hs + 1) * 256],
                              SWV, waits=[wvsl_free])
                    vg = None
                    for mt in range(8):
                        gm = half * 8 + mt
                        bank = [P0, P1][mt % 2]
                        for c in range(NCH):
                            w = [ln1_ready[half][c]]
                            if c == 0:
                                w += [v_w, mmfree[id(bank)]]
                            vg = mm(bank[:, 0:256], ln1v(c, mt * 128, 128),
                                    WVSL[:, c * 256:(c + 1) * 256],
                                    c == 0, c == NCH - 1,
                                    waits=w, inc=s_pe if c == NCH - 1 else None)
                        v_e = acp(vsb(gm, hs * 256, 256), bank[:, 0:256],
                                  waits=[(s_pe, vg)])
                        mmfree[id(bank)] = (s_act, v_e)
                    wvsl_free = (s_pe, vg)
                    v_last = vg
                wpc_free = [(s_pe, v_last), (s_pe, v_last)]
                qkv_done[half] = v_last

        phase1_pe = (s_pe, s_pe.n)
        phase1_act = (s_act, s_act.n)

        if "ksb" in dump_d:
            for h in range(H):
                dma("sync", dump_d["ksb"][h * 128:(h + 1) * 128, :], ksb(h),
                    SDBG, waits=[phase1_act, (SDBG, SDBG.n)])
        if "vsb" in dump_d:
            for mt in range(16):
                dma("sync", dump_d["vsb"][mt * 128:(mt + 1) * 128, :], vsb(mt),
                    SDBG, waits=[phase1_act, (SDBG, SDBG.n)])
        if "attn" in phases:
            q_free = [phase1_pe] * 4
            pt_free = [phase1_pe] * 6
            z_reg_free = phase1_pe
            recip_free = (SBO[0], SBO[0].n) if SBO[0].n else abc_free
            lns_free = (SBO[1], SBO[1].n) if SBO[1].n else None
            x2st_free = [xs_free[2], xs_free[3], xs_free[6], xs_free[7]]
            wob_free = [wpc_free[0], wpc_free[1]]
            sb_free = {id(P0): mmfree[id(P0)], id(P1): mmfree[id(P1)],
                       id(P7): None}
            sum_free = {id(P4): p4_free, id(P5): p5_free}
            zb_free = {id(P2): p2_free, id(P3): None}
            wore = wo_bf.rearrange("(hc p) d -> p hc d", p=128)
            wob_ready = wcv_state["out"]
            kready = phase1_act
            SCB = [P0, P1, P7]

            for qb in range(4):
                nmt = 4 * qb + 4
                qcols = qb * 512
                sstep = 0
                ob_free = [sb_free[id(P0)], sb_free[id(P1)],
                           zb_free[id(P3)], sb_free[id(P7)]]
                qpend = {}

                def qload(g):
                    for idx, h in enumerate([2 * g, 2 * g + 1]):
                        sl = (2 * g + idx) % 4
                        qpend[h] = (sl, dma("sync", QSL[sl],
                                    q_hbm[h * 128:(h + 1) * 128,
                                          qcols:qcols + 512],
                                    SQL[sl], waits=[q_free[sl], qstore[(h, qb)]]))

                qload(0)
                for g in range(H // 2):
                    if g + 1 < H // 2:
                        qload(g + 1)
                    hh = [2 * g, 2 * g + 1]
                    hst = {}
                    for idx, h in enumerate(hh):
                        sl, v_q = qpend.pop(h)
                        hst[h] = dict(q=v_q, qsl=sl,
                                      zb=[P3, P2][idx], sumb=[P4, P5][idx],
                                      pts=[PT[3 * idx + i] for i in range(3)],
                                      ptbase=3 * idx,
                                      zg=None, sg=None, ready={})

                    def emit_S(h, mt):
                        nonlocal sstep
                        s_ = hst[h]
                        off = mt - 4 * qb
                        qlo = max(0, off * 128)
                        sbank = SCB[sstep % 3]
                        sstep += 1
                        diag = off >= 0
                        v_s = mm(sbank[:, qlo:512], ksb(h, mt * 128, 128),
                                 QSL[s_["qsl"]][:, qlo:512], True, not diag,
                                 waits=[s_["q"], kready, sb_free[id(sbank)]],
                                 inc=None if diag else s_pe)
                        if diag:
                            v_s = mm(sbank[:, qlo:qlo + 128], ident, trineg,
                                     False, True, inc=s_pe)
                        ptb = s_["ptbase"] + mt % 3
                        v_e = aact(PT[ptb][:, qlo:512], sbank[:, qlo:512],
                                   AF.Exp, waits=[(s_pe, v_s), pt_free[ptb]])
                        sb_free[id(sbank)] = (s_act, v_e)
                        s_["ready"][mt] = (qlo, ptb, (s_act, v_e), v_s)

                    def emit_PV(h, mt):
                        s_ = hst[h]
                        qlo, ptb, ptr, v_s = s_["ready"].pop(mt)
                        zbank = s_["zb"]
                        sumb = s_["sumb"]
                        wz = [ptr]
                        if mt == 0:
                            wz.append(zb_free[id(zbank)])
                        s_["zg"] = mm(zbank[:, qlo:512], vsb(mt, h * 128, 128),
                                      PT[ptb][:, qlo:512], mt == 0, mt == nmt - 1,
                                      waits=wz, inc=s_pe if mt == nmt - 1 else None)
                        ws = []
                        if mt == 0:
                            ws.append(sum_free[id(sumb)])
                        s_["sg"] = mm(sumb[:, qlo:512], ones128,
                                      PT[ptb][:, qlo:512], mt == 0, mt == nmt - 1,
                                      waits=ws, inc=s_pe)
                        pt_free[ptb] = (s_pe, s_["sg"])

                    emit_S(hh[0], 0)
                    emit_S(hh[1], 0)
                    for mt in range(nmt):
                        if mt + 1 < nmt:
                            emit_S(hh[0], mt + 1)
                        emit_PV(hh[0], mt)
                        if mt + 1 < nmt:
                            emit_S(hh[1], mt + 1)
                        emit_PV(hh[1], mt)

                    for idx, h in enumerate(hh):
                        s_ = hst[h]
                        zbank = s_["zb"]
                        sumb = s_["sumb"]
                        q_free[s_["qsl"]] = (s_pe, s_["sg"])
                        if qb < 2:
                            v_l = aact(LNS, sumb[:, :], AF.Ln,
                                       waits=[(s_pe, s_["sg"]), lns_free])
                            sum_free[id(sumb)] = (s_act, v_l)
                            v_r = aact(ABC, LNS, AF.Exp, scale=-1.0,
                                       waits=[recip_free, (s_act, v_l)])
                            lns_free = (s_act, v_r)
                            rwait = (s_act, v_r)
                        else:
                            v_r = vrecip(ABC, sumb[:, :],
                                         waits=[(s_pe, s_["sg"]), recip_free])
                            sum_free[id(sumb)] = (s_dve, v_r)
                            rwait = (s_dve, v_r)
                        v_zn = vtt(zv(h), zbank[:, :], ABC, ALU.mult,
                                   waits=[(s_pe, s_["zg"]), rwait,
                                          z_reg_free])
                        recip_free = (s_dve, v_zn)
                        zb_free[id(zbank)] = (s_dve, v_zn)
                        z_ready_last = v_zn
                v_og = None
                for dt in range(16):
                    wb_ = dt % 2
                    v_w = dma("gpsimd",
                              WPC[wb_].rearrange("p (c f) -> p c f", f=128),
                              wore[:, :, dt * 128:(dt + 1) * 128],
                              SWP[wb_], waits=[wob_free[wb_], wob_ready])
                    xsl = [0, 1, 4, 5][dt % 4]
                    v_x = dma("sync", XS[xsl],
                              x_d[dt * 128:(dt + 1) * 128, qcols:qcols + 512],
                              SX[xsl], waits=[xs_free[xsl], xs_free2[xsl]])
                    obank = [P0, P1, P3, P7][dt % 4]
                    vg = None
                    for hc in range(H):
                        w = []
                        if hc == 0:
                            w = [v_w, (s_dve, z_ready_last),
                                 ob_free[dt % 4]]
                        vg = mm(obank[:, :], WPC[wb_][:, hc * 128:(hc + 1) * 128],
                                zv(hc), hc == 0, hc == H - 1,
                                waits=w, inc=s_pe if hc == H - 1 else None)
                    v_og = vg
                    wob_free[wb_] = (s_pe, vg)
                    st = [2, 3, 6, 7][dt % 4]
                    v_ev = vtt(XS[st], obank[:, :], XS[xsl], ALU.add,
                               waits=[(s_pe, vg), v_x, x2st_free[dt % 4]])
                    ob_free[dt % 4] = (s_dve, v_ev)
                    xs_free[xsl] = (s_dve, v_ev)
                    v_st = dma("scalar",
                               x2_hbm[dt * 128:(dt + 1) * 128, qcols:qcols + 512],
                               XS[st], SX[st], waits=[(s_dve, v_ev)])
                    x2st_free[dt % 4] = v_st
                    x2store[(dt, qb)] = v_st
                sb_free[id(P0)] = ob_free[0]
                sb_free[id(P1)] = ob_free[1]
                zb_free[id(P3)] = ob_free[2]
                sb_free[id(P7)] = ob_free[3]
                z_reg_free = (s_pe, v_og)
            mmfree[id(P0)] = sb_free[id(P0)]
            mmfree[id(P1)] = sb_free[id(P1)]
            p2_free = zb_free[id(P2)]
            p4_free = sum_free[id(P4)]
            p5_free = sum_free[id(P5)]
            abc_free = recip_free
            xs_free[2] = x2st_free[0]
            xs_free[3] = x2st_free[1]
            xs_free[6] = x2st_free[2]
            xs_free[7] = x2st_free[3]
            wpc_free = wob_free

        attn_pe = (s_pe, s_pe.n)
        attn_dve = (s_dve, s_dve.n)

        if "x2" in dump_d and "attn" in phases:
            wlast = [x2store[(dt, qb)] for dt in range(NCH) for qb in range(4)]
            wl = {}
            for s, v in wlast:
                wl[s.name] = (s, max(v, wl.get(s.name, (s, 0))[1]))
            wlast = list(wl.values())
            for c in range(NCH):
                for qb in range(4):
                    v_l = dma("sync", XS[0],
                              x2_hbm[c * 128:(c + 1) * 128, qb * 512:(qb + 1) * 512],
                              SX[0], waits=wlast + [xs_free[0]])
                    v_s = dma("sync",
                              dump_d["x2"][c * 128:(c + 1) * 128, qb * 512:(qb + 1) * 512],
                              XS[0], SX[0], waits=[v_l])
                    xs_free[0] = v_s

        if "ffn" in phases:
            wa_re = wa_d.rearrange("(c p) ar -> p c ar", p=128)
            wb_re = wb_d.rearrange("(cc p) j -> p cc j", p=128)
            ww_re = ww_d.rearrange("(c p) j -> p c j", p=128)
            wbar = [attn_pe, wpc_free[0], wpc_free[1]]
            a12_free = [attn_pe, attn_pe]
            DBANKS = [P0, P1, P3, P7]
            rexp_free = [attn_pe, attn_pe]
            vec_free = [attn_pe] * 8
            ln2_free = attn_pe
            a5_free = attn_dve
            a6_free = attn_pe
            a7_free = attn_pe
            a8_free = attn_pe
            p7_free = None
            topk_free = None
            last_out = None

            stat_state = {}

            def emit_stats_chunk(tb, c):
                nonlocal p4_free, p5_free
                tc = tb * 512
                sl = 4 + c % 4
                ssl = c % 2
                v_ld = dma("sync", XS[sl],
                           x2_hbm[c * 128:(c + 1) * 128, tc:tc + 512],
                           SX[sl], waits=[xs_free[sl], xs_free2[sl], x2store[(c, tb)]])
                v_s2 = aact(XS[ssl], XS[sl], AF.Square,
                            waits=[v_ld, xs_free[ssl]])
                v_sum = mm(P4[:, :], ones128f, XS[sl], c == 0, c == NCH - 1,
                           waits=[v_ld] + ([p4_free, consts_ready] if c == 0 else []),
                           inc=s_pe)
                xs_free[sl] = (s_pe, v_sum)
                xs_free2[sl] = (s_act, v_s2)
                v_sq = mm(P5[:, :], ones128f, XS[ssl], c == 0, c == NCH - 1,
                          waits=[(s_act, v_s2)] + ([p5_free] if c == 0 else []),
                          inc=s_pe)
                xs_free[ssl] = (s_pe, v_sq)
                stat_state[tb] = (v_sum, v_sq)

            v_wa = dma("gpsimd", BIG3[:, 0:16384].rearrange("p (c f) -> p c f", f=1024),
                       wa_re, SWA, waits=wbar)
            v_wb = dma("gpsimd", BIG3[:, 16384:32768].rearrange("p (c f) -> p c f", f=2048),
                       wb_re, SWB, waits=wbar)
            for k in range(4):
                dma("gpsimd",
                    BIG2[:, k * 8192:(k + 1) * 8192].rearrange("p (c f) -> p c f", f=2048),
                    ww_re[:, 4 * k:4 * (k + 1), :], SWW, waits=wbar)
            v_ww = (SWW, SWW.n)

            dbank_free = [mmfree[id(P0)], mmfree[id(P1)],
                          zb_free[id(P3)] if "attn" in phases else None,
                          sb_free[id(P7)] if "attn" in phases else None]
            for tb in range(4):
                tc = tb * 512
                for c in range(NCH):
                    emit_stats_chunk(tb, c)
                v_abc, v_bbc = lnrows(*stat_state[tb])

                last_apply = None
                apply_chunk_v = [None] * NCH
                for c in range(NCH):
                    sl = 4 + c % 4
                    v_ld = dma("sync", XS[sl],
                               x2_hbm[c * 128:(c + 1) * 128, tc:tc + 512],
                               SX[sl], waits=[xs_free[sl], xs_free2[sl]])
                    scr = 2 + c % 2
                    v_t1 = vtt(XS[scr], XS[sl], RWF[2], ALU.mult,
                               waits=[v_ld, v_abc, xs_free[scr]])
                    xs_free[sl] = (s_dve, v_t1)
                    v_t2 = vtt(ln2v(c), XS[scr], RWF[3], ALU.subtract,
                               waits=[v_bbc, ln2_free, (s_dve, v_t1)])
                    xs_free[scr] = (s_dve, v_t2)
                    last_apply = v_t2
                    apply_chunk_v[c] = (s_dve, v_t2)
                ln2_ready = (s_dve, last_apply)
                abc_free = ln2_ready
                bbc_free = ln2_ready
                ln2_rdy_c = apply_chunk_v

                v_arr_g = None
                for at in range(8):
                    bank = [P0, P1][at % 2]
                    vg = None
                    for c in range(NCH):
                        w = [ln2_rdy_c[c]]
                        if c == 0:
                            w += [v_wa, mmfree[id(bank)]]
                        vg = mm(bank[:, :], wasb(c)[:, at * 128:(at + 1) * 128],
                                ln2v(c), c == 0, c == NCH - 1,
                                waits=w, inc=s_pe if c == NCH - 1 else None)
                    v_ev = acp(vecv(at), bank[:, :], waits=[(s_pe, vg), vec_free[at]])
                    v_s2 = aact([A1, A2][at % 2], bank[:, :], AF.Square,
                                waits=[a12_free[at % 2]])
                    mmfree[id(bank)] = (s_act, v_s2)
                    vec_free[at] = (s_act, v_ev)
                    v_arr_g = mm(P4[0:64, :], selA[at], [A1, A2][at % 2],
                                 at == 0, at == 7,
                                 waits=[(s_act, v_s2)] + ([p4_free] if at == 0 else []),
                                 inc=s_pe)
                    a12_free[at % 2] = (s_pe, v_arr_g)
                vecs_ready = {at: vec_free[at] for at in range(8)}

                v_ae = acp(XS[0][0:64, :], P4[0:64, :],
                           waits=[(s_pe, v_arr_g), xs_free[0]])
                p4_free = (s_act, v_ae)
                v_tes = []
                v_tr = None
                for sl4 in range(4):
                    v_tr = tr(P2[:, 0:64], XS[0][0:64, sl4 * 128:(sl4 + 1) * 128],
                              identf64, waits=[(s_act, v_ae), p2_free])
                    v_te = acp(XS[1][:, sl4 * 64:(sl4 + 1) * 64], P2[:, 0:64],
                               waits=[(s_pe, v_tr)] + ([xs_free[1]] if sl4 == 0 else []))
                    p2_free = (s_act, v_te)
                    v_tes.append(v_te)
                xs_free[0] = (s_pe, v_tr)

                wpart = {}
                for jt in range(4):
                    bank = DBANKS[jt % 4]
                    vg = None
                    for c in range(NCH):
                        w = [ln2_rdy_c[c]]
                        if c == 0:
                            w += [v_ww, dbank_free[jt % 4]]
                        vg = mm(bank[:, :], wwsb(c)[:, jt * 128:(jt + 1) * 128],
                                ln2v(c), c == 0, False, waits=w)
                    wpart[jt] = True


                MSL = [TOPK[:, 576 + i:577 + i] for i in range(4)]
                INDT = TOPK[:, 512:576]
                RTT = TOPK[:, 640:704]
                NX = [[TOPK[:, s * 128:s * 128 + 64],
                       TOPK[:, s * 128 + 64:s * 128 + 128]] for s in range(4)]
                cur = {}
                vprev = None
                for s in range(4):
                    x0 = XS[1][:, s * 64:(s + 1) * 64]
                    vprev = vmax(MSL[s], x0, waits=[(s_act, v_tes[s])] +
                                 ([topk_free] if s == 0 else []))
                    cur[s] = x0
                for k in range(3):
                    for s in range(4):
                        vprev = vts(INDT, cur[s], MSL[s], ALU.is_ge,
                                    waits=[(s_dve, vprev)])
                        nxt = NX[s][k % 2]
                        vprev = vstt(nxt, INDT, -1e9, cur[s], ALU.mult, ALU.add,
                                     waits=[(s_dve, vprev)])
                        vprev = vmax(MSL[s], nxt, waits=[(s_dve, vprev)])
                        cur[s] = nxt
                rt_free = None
                v_rfm = None
                v_rt_last = None
                for s in range(4):
                    x0 = XS[1][:, s * 64:(s + 1) * 64]
                    v_r01 = vts(RTT, x0, MSL[s], ALU.is_ge,
                                waits=[(s_dve, vprev)] + ([rt_free] if rt_free else []))
                    vprev = v_r01
                    v_rt_last = v_r01
                    v_tr2 = tr(P2[0:64, 0:128], RTT, identf128,
                               waits=[(s_dve, v_r01), p2_free])
                    rt_free = (s_pe, v_tr2)
                    v_rfm = acp(A6[0:64, s * 128:(s + 1) * 128],
                                P2[0:64, 0:128],
                                waits=[(s_pe, v_tr2)] + ([a6_free] if s == 0 else []))
                    p2_free = (s_act, v_rfm)
                topk_free = rt_free
                xs_free[1] = (s_dve, v_rt_last)

                if "arr" in dump_d:
                    v_da = dma("sync", dump_d["arr"][:, tc:tc + 512],
                               XS[0][0:64, :], SDBG,
                               waits=[(s_act, v_ae), (SDBG, SDBG.n)])
                    xs_free[0] = v_da
                    v_dt = dma("sync", dump_d["atm"][:, tb * 256:(tb + 1) * 256],
                               XS[1][:, 0:256], SDBG,
                               waits=[(s_act, v_tes[-1]), (SDBG, SDBG.n)])
                    v_dr = dma("sync", dump_d["routes"][:, tc:tc + 512],
                               A6[0:64, :], SDBG,
                               waits=[(s_act, v_rfm), (SDBG, SDBG.n)])
                    a6_free = v_dr

                PB = [P6, P2]
                pbf = [p6_free, p2_free]
                v01g = None
                for cc in range(8):
                    v_re = mm(PB[cc % 2][:, :], selExp[cc], A6[0:64, :], True, True,
                              waits=[(s_act, v_rfm), pbf[cc % 2]], inc=s_pe)
                    v_ree = acp([A3, A4][cc % 2], PB[cc % 2][:, :],
                                waits=[(s_pe, v_re), rexp_free[cc % 2]])
                    pbf[cc % 2] = (s_act, v_ree)
                    v_mk = vtt([A1, A2][cc % 2], vecv(cc), [A3, A4][cc % 2], ALU.mult,
                               waits=[(s_act, v_ree), vecs_ready[cc], a12_free[cc % 2]])
                    rexp_free[cc % 2] = (s_dve, v_mk)
                    v01g = mm(P5[0:16, :], selR, [A1, A2][cc % 2], cc == 0, cc == 7,
                              waits=[(s_dve, v_mk)] + ([p5_free] if cc == 0 else []),
                              inc=s_pe)
                    a12_free[cc % 2] = (s_pe, v01g)

                v_vq = acp(A8[0:16, :], P5[0:16, :],
                           waits=[(s_pe, v01g), a8_free], scale=0.0625)
                p5_free = (s_act, v_vq)
                v_vx = mm(P6[:, :], selTile, A8[0:16, :], True, True,
                          waits=[(s_act, v_vq), pbf[0]], inc=s_pe)
                a8_free = (s_pe, v_vx)
                v_ve = acp(A5, P6[:, :], waits=[(s_pe, v_vx), a5_free])
                pbf[0] = (s_act, v_ve)

                u_ready = None
                v_re = None
                for cc in range(8):
                    v_re = mm(PB[cc % 2][:, :], selExp[cc], A6[0:64, :], True, True,
                              waits=[pbf[cc % 2]], inc=s_pe)
                    v_ree = acp([A3, A4][cc % 2], PB[cc % 2][:, :],
                                waits=[(s_pe, v_re), rexp_free[cc % 2]])
                    pbf[cc % 2] = (s_act, v_ree)
                    v_u = vtt(vecv(cc), [A3, A4][cc % 2], A5, ALU.mult,
                              waits=[(s_act, v_ree), (s_act, v_ve), vec_free[cc]])
                    rexp_free[cc % 2] = (s_dve, v_u)
                    vec_free[cc] = (s_dve, v_u)
                    u_ready = v_u
                a6_free = (s_pe, v_re)
                p6_free = pbf[0]
                p2_free = pbf[1]
                a5_free = (s_dve, u_ready)
                if "u" in dump_d:
                    for cc in range(8):
                        v_du = dma("sync",
                                   dump_d["u"][cc * 128:(cc + 1) * 128, tc:tc + 512],
                                   vecv(cc), SDBG,
                                   waits=[(s_dve, u_ready), (SDBG, SDBG.n)])
                        vec_free[cc] = v_du

                for jt in range(16):
                    bank = DBANKS[jt % 4]
                    xsl = [0, 1, 4, 5][jt % 4]
                    v_x2 = dma("sync", XS[xsl],
                               x2_hbm[jt * 128:(jt + 1) * 128, tc:tc + 512],
                               SX[xsl], waits=[xs_free[xsl], xs_free2[xsl],
                                               x2store[(jt, tb)]])
                    if jt not in wpart:
                        vg = None
                        for c in range(NCH):
                            w = []
                            if c == 0:
                                w = [dbank_free[jt % 4]]
                            vg = mm(bank[:, :], wwsb(c)[:, jt * 128:(jt + 1) * 128],
                                    ln2v(c), c == 0, False, waits=w)
                    vg = None
                    for cc in range(8):
                        w = []
                        if cc == 0:
                            w = [(s_dve, u_ready), v_wb]
                        vg = mm(bank[:, :], wbsb(cc)[:, jt * 128:(jt + 1) * 128],
                                vecv(cc), False, cc == 7, waits=w,
                                inc=s_pe if cc == 7 else None)
                    st = [2, 3, 6, 7][jt % 4]
                    v_o = vstt(XS[st], bank[:, :], BIASR[:, jt:jt + 1], XS[xsl],
                               ALU.add, ALU.add,
                               waits=[(s_pe, vg), v_x2, xs_free[st]])
                    dbank_free[jt % 4] = (s_dve, v_o)
                    xs_free[xsl] = (s_dve, v_o)
                    v_st = dma("scalar", out_d[jt * 128:(jt + 1) * 128, tc:tc + 512],
                               XS[st], SX[st], waits=[(s_dve, v_o)])
                    xs_free[st] = v_st
                    last_out = v_st
                mmfree[id(P0)] = dbank_free[0]
                mmfree[id(P1)] = dbank_free[1]

                ln2_free = (s_pe, s_pe.n)
                for cc in range(8):
                    vec_free[cc] = ln2_free



            final_wait = ("sync", [(SX[2], SX[2].n), (SX[3], SX[3].n)])
        else:
            final_wait = ("sync", [(s, s.n) for s in sems if s.n > 0
                                   and s not in (s_pe, s_act, s_dve)])

        LAST_PLAN = (p, final_wait)
        with nc.Block() as block:
            def mkrunner(engname):
                def r(eng):
                    waited = {}
                    for fn, waits, inc, n in p.q[engname]:
                        for s, v in waits:
                            if waited.get(s.name, 0) < v:
                                eng.wait_ge(s.h, v)
                                waited[s.name] = v
                        inst = fn(eng)
                        if inc is not None:
                            inst.then_inc(inc.h, n)
                    if engname == final_wait[0]:
                        for fs, fv in final_wait[1]:
                            if fv > 0:
                                eng.wait_ge(fs.h, fv)
                return r

            block.sync(mkrunner("sync"))
            block.tensor(mkrunner("tensor"))
            block.vector(mkrunner("vector"))
            block.scalar(mkrunner("scalar"))
            block.gpsimd(mkrunner("gpsimd"))

    return nc


def _consts():
    cb = np.zeros((128, 2304), dtype=np.float32)
    cb[:, 0:128] = np.eye(128, dtype=np.float32)
    for m in range(128):
        cb[m, 128 + m:256] = 1.0
    for c in range(8):
        for pp in range(128):
            cb[pp, 256 + c * 64 + 8 * c + pp // 16] = 1.0
    for c in range(8):
        for i in range(128):
            a = 8 * c + i // 16
            cb[a, 768 + c * 128 + i] = 1.0
    for pp in range(128):
        cb[pp, 1792 + pp % 16] = 1.0
    for i in range(128):
        cb[i % 16, 1808 + i] = 1.0
    cb[:, 1936] = 1.0
    for m in range(128):
        cb[m, 2048:2048 + m] = -1e9
    cb[:, 2176:2304] = 1.0
    cf = np.zeros((128, 512), dtype=np.float32)
    cf[0, 0:128] = 1.0
    cf[0:64, 128:192] = np.eye(64, dtype=np.float32)
    cf[:, 192] = EPS
    cf[:, 256:384] = np.eye(128, dtype=np.float32)
    cf[:, 384:512] = 1.0
    return cb.astype(ml_dtypes.bfloat16), cf


def make_in_maps(L0_attn_x, L0_attn_W_Q, L0_attn_W_K, L0_attn_W_V, L0_attn_W_O,
                 L0_ffn_A, L0_ffn_B, L0_ffn_W, L0_ffn_bias):
    x = np.asarray(L0_attn_x, dtype=np.float32)
    wq = np.ascontiguousarray(np.asarray(L0_attn_W_Q, dtype=np.float32).reshape(D, D).T)
    wk = np.ascontiguousarray(np.asarray(L0_attn_W_K, dtype=np.float32).reshape(D, D).T)
    wv = np.ascontiguousarray(np.asarray(L0_attn_W_V, dtype=np.float32).reshape(D, D).T)
    wo = np.ascontiguousarray(
        np.asarray(L0_attn_W_O, dtype=np.float32).transpose(0, 2, 1).reshape(D, D))
    wa = np.ascontiguousarray(np.asarray(L0_ffn_A, dtype=np.float32).reshape(AR, D).T)
    wb = np.ascontiguousarray(
        np.asarray(L0_ffn_B, dtype=np.float32).transpose(0, 2, 1).reshape(AR, D))
    ww = np.ascontiguousarray(np.asarray(L0_ffn_W, dtype=np.float32).T)
    biasr = np.ascontiguousarray(
        np.asarray(L0_ffn_bias, dtype=np.float32).reshape(16, 128).T)
    cb, cf = _consts()
    shared = dict(wq=wq, wk=wk, wv=wv, wo=wo, wa=wa, wb=wb, ww=ww,
                  biasr=biasr, constb=cb, constf=cf)
    maps = []
    for b in range(8):
        m = dict(shared)
        m["x"] = np.ascontiguousarray(x[b].T)
        maps.append(m)
    return maps


def kernel(L0_attn_x, L0_attn_W_Q, L0_attn_W_K, L0_attn_W_V, L0_attn_W_O,
           L0_ffn_A, L0_ffn_B, L0_ffn_W, L0_ffn_bias):
    maps = make_in_maps(L0_attn_x, L0_attn_W_Q, L0_attn_W_K, L0_attn_W_V,
                        L0_attn_W_O, L0_ffn_A, L0_ffn_B, L0_ffn_W, L0_ffn_bias)
    nc = build_nc()
    res = run_bass_kernel_spmd(nc, maps, core_ids=list(range(8)))
    out = np.stack([np.ascontiguousarray(np.asarray(r["out"], dtype=np.float32).T)
                    for r in res.results])
    return out
